# version 07
# speedup vs baseline: 2.4886x; 1.1728x over previous
import math
from bisect import bisect_right
from contextlib import ExitStack

import numpy as np
import ml_dtypes

import concourse.bass as bass
import concourse.mybir as mybir
from concourse.bass_utils import run_bass_kernel_spmd

F32 = mybir.dt.float32
BF16 = mybir.dt.bfloat16
AF = mybir.ActivationFunctionType
ALU = mybir.AluOpType
AX = mybir.AxisListType

NCORES = 8
S = 8192
D = 2048
TCORE = S // NCORES
DC = D // 128
TT = 512
NTT = TCORE // TT
F = 8192
FC = F // 128
E = 8
RP = 32
H = 16
GKV = 4
DH = 128
CH0 = 128
NCH0 = TCORE // CH0
CH1 = 128
NCH1 = TCORE // CH1
NEX = 6
EI = 1024
LN_EPS = 1e-5
ATT_EPS = 1e-6
BIG = 1e30

ENG2BLK = {"pe": "tensor", "act": "scalar", "dve": "vector", "pool": "gpsimd", "sp": "sync"}
NDMALANES = 8
DMA_QUEUES = ("sp", "act", "pool")



class Op:
    __slots__ = ("eng", "emit", "is_dma", "seq", "wait_sites", "dbg")

    def __init__(self, eng, emit, is_dma, seq, wait_sites, dbg):
        self.eng = eng
        self.emit = emit
        self.is_dma = is_dma
        self.seq = seq
        self.wait_sites = wait_sites
        self.dbg = dbg


def _rng_overlap(a0, a1, b0, b1):
    return a0 < b1 and b0 < a1


class Builder:

    def __init__(self, nc):
        self.nc = nc
        self.ops = []
        self.cseq = {e: 0 for e in ENG2BLK}
        self.dseq = {q: 0 for q in DMA_QUEUES}
        self.acc = {}
        self.handles = {}
        self.psum_ids = set()

    @staticmethod
    def v(t, ps=None, fs=None):
        shp = list(t.shape)
        p0, p1 = (0, shp[0]) if ps is None else ps
        f0, f1 = (0, int(np.prod(shp[1:]))) if fs is None else fs
        return (t, p0, p1, f0, f1)

    def _conflict_sites(self, view, is_write):
        t, p0, p1, f0, f1 = view
        key = id(t)
        if key in self.psum_ids:
            p0, p1, f0, f1 = 0, 1 << 30, 0, 1 << 30
        self.handles[key] = t
        lst = self.acc.get(key)
        sites = []
        if lst:
            for (w, q0, q1, g0, g1, site) in lst:
                if (w or is_write) and _rng_overlap(p0, p1, q0, q1) and _rng_overlap(f0, f1, g0, g1):
                    sites.append(site)
        return sites

    def _record_access(self, view, is_write, site):
        t, p0, p1, f0, f1 = view
        key = id(t)
        if key in self.psum_ids:
            p0, p1, f0, f1 = 0, 1 << 30, 0, 1 << 30
        lst = self.acc.setdefault(key, [])
        if is_write and p0 == 0 and f0 == 0:
            shp = list(t.shape)
            if p1 >= shp[0] and f1 >= int(np.prod(shp[1:])):
                lst.clear()
        lst.append((is_write, p0, p1, f0, f1, site))
        if len(lst) > 400:
            del lst[: len(lst) - 260]

    def op(self, eng, emit, reads=(), writes=(), dbg=""):
        sites = []
        for r in reads:
            sites.extend(self._conflict_sites(r, False))
        for w in writes:
            sites.extend(self._conflict_sites(w, True))
        my_site = ("c", eng, self.cseq[eng] + 1)
        keep = []
        seen = set()
        for s in sites:
            if eng == "pe" and s[0] == "c" and s[1] == "pe":
                continue
            if s in seen:
                continue
            seen.add(s)
            keep.append(s)
        self.cseq[eng] += 1
        o = Op(eng, emit, False, self.cseq[eng], keep, dbg)
        self.ops.append(o)
        for r in reads:
            self._record_access(r, False, my_site)
        for w in writes:
            self._record_access(w, True, my_site)
        return o

    def dma(self, q, dst, src, dbg="", emit=None):
        if emit is None:
            dst_ap = self.ap(dst)
            src_ap = self.ap(src)
            emit = lambda e, d=dst_ap, s=src_ap: e.dma_start(out=d, in_=s)
        sites = []
        sites.extend(self._conflict_sites(src, False))
        sites.extend(self._conflict_sites(dst, True))
        my_site = ("d", q, self.dseq[q] + 1)
        keep = []
        seen = set()
        for s in sites:
            if s in seen:
                continue
            seen.add(s)
            keep.append(s)
        self.dseq[q] += 1
        if self.dseq[q] > NDMALANES:
            keep.append(("d", q, self.dseq[q] - NDMALANES))
        o = Op(q, emit, True, self.dseq[q], keep, dbg)
        self.ops.append(o)
        self._record_access(src, False, my_site)
        self._record_access(dst, True, my_site)
        return o

    @staticmethod
    def ap(view):
        t, p0, p1, f0, f1 = view
        shp = list(t.shape)
        if len(shp) == 2:
            return t[p0:p1, f0:f1]
        flat = t.rearrange(f"a b c -> a (b c)") if len(shp) == 3 else t
        return flat[p0:p1, f0:f1]

    def barrier(self):
        sites = []
        for e in ENG2BLK:
            if self.cseq[e]:
                sites.append(("c", e, self.cseq[e]))
        for q in DMA_QUEUES:
            for idx in range(max(1, self.dseq[q] - NDMALANES + 1), self.dseq[q] + 1):
                sites.append(("d", q, idx))
        for e in ENG2BLK:
            my = [s for s in sites if not (s[0] == "c" and s[1] == e)]
            self.cseq[e] += 1
            self.ops.append(Op(e, lambda eng: eng.nop(), False, self.cseq[e], my, "barrier"))
        self.acc.clear()

    def emit_all(self, es):
        nc = self.nc
        cwaits = {e: set() for e in ENG2BLK}
        dwaits = {q: set() for q in DMA_QUEUES}
        for o in self.ops:
            for s in o.wait_sites:
                if s[0] == "c":
                    cwaits[s[1]].add(s[2])
                else:
                    dwaits[s[1]].add(s[2])
        cms = {e: sorted(cwaits[e]) for e in ENG2BLK}
        dms = {q: sorted(dwaits[q]) for q in DMA_QUEUES}
        csem = {e: es.enter_context(nc.semaphore(f"cp_{e}")) for e in ENG2BLK}
        dsem = {q: [es.enter_context(nc.semaphore(f"dq_{q}{j}")) for j in range(NDMALANES)]
                for q in DMA_QUEUES}
        cm_set = {e: set(cms[e]) for e in ENG2BLK}
        dm_set = {q: set(dms[q]) for q in DMA_QUEUES}

        per_eng = {e: [] for e in ENG2BLK}
        for o in self.ops:
            per_eng[o.eng].append(o)

        block = es.enter_context(nc.Block())

        def run_engine(ename):
            ops = per_eng[ename]
            watermark = {}

            def body(eng):
                for o in ops:
                    for s in o.wait_sites:
                        if s[0] == "c":
                            sem = csem[s[1]]
                            val = bisect_right(cms[s[1]], s[2])
                        else:
                            sem = dsem[s[1]][(s[2] - 1) % NDMALANES]
                            val = 16 * ((s[2] - 1) // NDMALANES + 1)
                        k = id(sem)
                        if watermark.get(k, -1) < val:
                            eng.wait_ge(sem, val)
                            watermark[k] = val
                    inst = o.emit(eng)
                    if o.is_dma:
                        inst.then_inc(dsem[o.eng][(o.seq - 1) % NDMALANES], 16)
                    else:
                        if o.seq in cm_set[o.eng]:
                            inst.then_inc(csem[o.eng], 1)
            return body

        getattr(block, ENG2BLK["pe"])(run_engine("pe"))
        getattr(block, ENG2BLK["act"])(run_engine("act"))
        getattr(block, ENG2BLK["dve"])(run_engine("dve"))
        getattr(block, ENG2BLK["pool"])(run_engine("pool"))
        getattr(block, ENG2BLK["sp"])(run_engine("sp"))



def _f32(a):
    return np.ascontiguousarray(np.asarray(a), dtype=np.float32)


def pack_lhsT(W, mb=128):
    W = _f32(W)
    K, M = W.shape
    kc = min(K, 128)
    nk = K // kc
    assert K % kc == 0 and M % mb == 0
    return np.ascontiguousarray(
        W.reshape(nk, kc, M // mb, mb).transpose(2, 1, 0, 3).reshape(M // mb, kc, nk * mb)
    )


def pad_rows(a, rows):
    out = np.zeros((rows, a.shape[1]), np.float32)
    out[: a.shape[0]] = a
    return out



def host_tables():
    t = {}
    dh = DH
    inv = (1.0 / (10000.0 ** (np.arange(0, dh, 2, dtype=np.float32) / dh))).astype(np.float32)
    pos = np.arange(S, dtype=np.float32)
    fr = pos[:, None] * inv[None, :]
    cos = np.concatenate([np.cos(fr), np.cos(fr)], axis=1).astype(np.float32)
    sin = np.concatenate([np.sin(fr), np.sin(fr)], axis=1).astype(np.float32)
    t["ropecos"] = np.ascontiguousarray(cos.T)
    t["ropesin"] = np.ascontiguousarray(sin.T)
    lam = (2.0 ** (-8.0 * np.arange(1, H + 1, dtype=np.float64) / H))
    r = np.arange(CH0, dtype=np.float64)
    diag = np.where(r[:, None] - r[None, :] >= 0,
                    np.exp(-lam[:, None, None] * (r[:, None] - r[None, :])), 0.0)
    diagT = diag.transpose(0, 2, 1)
    dT = np.zeros((128, H * CH0), np.float32)
    for h in range(H):
        dT[:, h * CH0:(h + 1) * CH0] = diagT[h].astype(np.float32)
    t["diagT"] = dT.astype(ml_dtypes.bfloat16)
    qdec = np.exp(-lam[:, None] * (r[None, :] + 1.0))
    t["qdec"] = np.broadcast_to(qdec.reshape(1, H * CH0), (128, H * CH0)).astype(ml_dtypes.bfloat16)
    kdec = np.exp(-lam[:, None] * (CH0 - 1.0 - r[None, :]))
    t["kdeccol"] = np.ascontiguousarray(kdec.T.astype(np.float32))
    t["blk"] = np.exp(-lam * CH0)
    t["lam"] = lam
    idx = np.arange(CH1)
    t["trilT"] = (idx[None, :] >= idx[:, None]).astype(ml_dtypes.bfloat16)
    t["identb"] = np.eye(128, dtype=ml_dtypes.bfloat16)
    t["identf"] = np.eye(128, dtype=np.float32)
    t["ones_col_bf"] = np.ones((128, 1), ml_dtypes.bfloat16)
    t["ones_row_bf"] = np.ones((1, 128), ml_dtypes.bfloat16)
    oe = np.zeros((8, 8 * 128), ml_dtypes.bfloat16)
    for e in range(8):
        oe[e, e * 128:(e + 1) * 128] = 1
    t["onehotE"] = oe
    oq = np.zeros((8, 2 * 128), ml_dtypes.bfloat16)
    for q in range(2):
        for j in range(4):
            oq[q * 4 + j, q * 128 + 32 * j: q * 128 + 32 * j + 32] = 1
    t["onehotQ"] = oq
    return t


def core_tables(core):
    t = host_tables()
    sl = slice(core * TCORE, (core + 1) * TCORE)
    out = {
        "ropecos": np.ascontiguousarray(t["ropecos"][:, sl]),
        "ropesin": np.ascontiguousarray(t["ropesin"][:, sl]),
    }
    lam = t["lam"]
    c0 = np.zeros((128, NCORES * H), np.float32)
    for j in range(NCORES):
        for h in range(H):
            if j < core:
                c0[:, j * H + h] = float(np.exp(-lam[h] * TCORE * (core - 1 - j)))
    out["coefL0"] = c0
    c1 = np.zeros((128, NCORES), np.float32)
    c1[:, :core] = 1.0
    out["coefL1"] = c1
    return out



def pack_weights(i):
    w = {}
    w["l0_wqT"] = pack_lhsT(_f32(i["L0_attn_wq"]).T, 128).reshape(16 * 128, 2048)
    w["l0_wkT"] = pack_lhsT(_f32(i["L0_attn_wk"]).T, 128).reshape(4 * 128, 2048)
    w["l0_wvT"] = pack_lhsT(_f32(i["L0_attn_wv"]).T, 512).reshape(128, 16 * 512)
    w["l0_woT"] = pack_lhsT(_f32(i["L0_attn_wo"]).T, 128).reshape(16 * 128, 2048)
    W1 = _f32(i["L0_ffn_W1"])
    A2T = np.concatenate([_f32(i["L0_ffn_A2"][e]).T for e in range(E)], axis=1)
    A2Tp = np.zeros((F, E * RP), np.float32)
    for e in range(E):
        A2Tp[:, e * RP: e * RP + 16] = A2T[:, e * 16:(e + 1) * 16]
    B1q = []
    for q in range(2):
        B1q.append(np.concatenate(
            [pad_rows(_f32(i["L0_ffn_B1"][4 * q + j]).T, RP) for j in range(4)], axis=0))
    w1p = pack_lhsT(W1, 128)
    merged = np.zeros((FC, 128, 2560), np.float32)
    for fc in range(FC):
        merged[fc, :, :2048] = w1p[fc]
        merged[fc, :, 2048:2304] = A2Tp[fc * 128:(fc + 1) * 128, :]
        merged[fc, :, 2304:2432] = B1q[0][:, fc * 128:(fc + 1) * 128]
        merged[fc, :, 2432:2560] = B1q[1][:, fc * 128:(fc + 1) * 128]
    w["l0_w1blk"] = merged.reshape(FC * 128, 2560)
    A1T = np.concatenate([pad_rows(_f32(i["L0_ffn_A1"][e]), RP).T for e in range(E)], axis=1)
    w["l0_a1"] = pack_lhsT(A1T, 128).reshape(2 * 128, 2048)
    w["l0_w2blk"] = pack_lhsT(_f32(i["L0_ffn_W2"]), 128).reshape(16 * 128, 8192)
    w["l0_b2q"] = np.stack([
        np.concatenate([pad_rows(_f32(i["L0_ffn_B2"][4 * q + j]).T, RP) for j in range(4)], axis=0)
        for q in range(2)]).reshape(2 * 128, 2048)
    w["l0_wr"] = pack_lhsT(_f32(i["L0_ffn_Wr"]), 8).reshape(128, 128)
    w["l0_br8"] = _f32(i["L0_ffn_br"]).reshape(8, 1)
    w["l0_b2row"] = _f32(i["L0_ffn_b2"]).reshape(1, 2048)
    w["l1_wqT"] = pack_lhsT(_f32(i["L1_attn_Wq"]).T, 128).reshape(16 * 128, 2048)
    w["l1_wkT"] = pack_lhsT(_f32(i["L1_attn_Wk"]).T, 128).reshape(4 * 128, 2048)
    w["l1_wvT"] = pack_lhsT(_f32(i["L1_attn_Wv"]).T, 512).reshape(128, 16 * 512)
    w["l1_woT"] = pack_lhsT(_f32(i["L1_attn_Wo"]).T, 128).reshape(16 * 128, 2048)
    w["l1_wg"] = np.stack([pack_lhsT(_f32(i["L1_ffn_Wg"][j]), 8).reshape(128, 128)
                           for j in range(2)]).reshape(2 * 128, 128)
    w["l1_bg8"] = _f32(i["L1_ffn_bg"]).T
    w["l1_wup"] = np.stack([pack_lhsT(_f32(i["L1_ffn_Wup"][n]), 128).reshape(8 * 128, 2048)
                            for n in range(NEX)]).reshape(NEX * 8 * 128, 2048)
    w["l1_wsw"] = np.stack([pack_lhsT(_f32(i["L1_ffn_Wsw"][n]), 128).reshape(8 * 128, 1024)
                            for n in range(NEX)]).reshape(NEX * 8 * 128, 1024)
    w["l1_wdn"] = np.stack([pack_lhsT(_f32(i["L1_ffn_Wdn"][n]), 128).reshape(16 * 128, 1024)
                            for n in range(NEX)]).reshape(NEX * 16 * 128, 1024)
    bdnD = np.zeros((NEX, 8, 2048), np.float32)
    for n in range(NEX):
        bdnD[n, n, :] = _f32(i["L1_ffn_bdn"][n])
    w["l1_bdnD"] = bdnD.reshape(NEX * 8, 2048)
    w["l1_wsu"] = pack_lhsT(_f32(i["L1_ffn_Wsu"]), 128).reshape(16 * 128, 2048)
    w["l1_wsa"] = pack_lhsT(_f32(i["L1_ffn_Wsa"]), 128).reshape(8 * 128, 1024)
    w["l1_wsd"] = pack_lhsT(_f32(i["L1_ffn_Wsd"]), 128).reshape(16 * 128, 2048)
    w["l1_bsdrow"] = _f32(i["L1_ffn_bsd"]).reshape(1, 2048)
    return w



class MK:
    def __init__(self, launch):
        self.launch = launch
        self.nc = bass.Bass("TRN2", target_bir_lowering=False)
        self.b = Builder(self.nc)
        self.es = ExitStack()
        self.t = {}
        self.d = {}
        self.p = {}

    def dram_in(self, name, shape, dtype=F32):
        self.d[name] = self.nc.dram_tensor(name, list(shape), dtype, kind="ExternalInput")
        return self.d[name]

    def dram_out(self, name, shape, dtype=F32):
        self.d[name] = self.nc.dram_tensor(name, list(shape), dtype, kind="ExternalOutput")
        return self.d[name]

    def dram_int(self, name, shape, dtype=F32):
        self.d[name] = self.nc.dram_tensor(name, list(shape), dtype)
        return self.d[name]

    def sbuf(self, name, shape, dtype):
        self.t[name] = self.es.enter_context(self.nc.sbuf_tensor(name, list(shape), dtype))
        return self.t[name]

    def psum(self, name, shape, dtype=F32):
        self.p[name] = self.es.enter_context(self.nc.psum_tensor(name, list(shape), dtype))
        self.b.psum_ids.add(id(self.p[name]))
        return self.p[name]

    def alloc_common(self):
        s = self.sbuf
        ara = s("ARA", [128, 16384], F32)
        arb = s("ARB", [128, 8192], F32)
        arc = s("ARC", [128, 8192], F32)
        ard = s("ARD", [128, 4096], F32)
        self.t["ARAb"] = ara.bitcast(BF16)
        self.t["ARBb"] = arb.bitcast(BF16)
        self.t["ARCb"] = arc.bitcast(BF16)
        self.t["ARDb"] = ard.bitcast(BF16)
        s("SW0", [128, 2048], BF16)
        s("SW1", [128, 2048], BF16)
        s("SV0", [128, 2560], BF16)
        s("SV1", [128, 2560], BF16)
        s("XCH", [128, 2048], F32)
        s("SQB", [128, 4096], BF16)
        s("RSB", [128, 1024], BF16)
        s("MRSB", [128, 1024], BF16)
        s("GFM", [8, 1024], F32)
        s("GTM", [128, 64], F32)
        s("GSC", [128, 64], F32)
        s("CFM", [8, 1024], BF16)
        s("identb", [128, 128], BF16)
        s("identf", [128, 128], F32)
        s("ones_col", [128, 1], BF16)
        s("ones_row", [1, 512], BF16)
        s("onehotE", [8, 1024], BF16)
        s("onehotQ", [8, 256], BF16)
        s("biasfm", [128, 184], F32)
        s("browbuf", [1, 2048], BF16)
        s("bdnbuf", [8, 2048], BF16)
        s("br8", [8, 1], F32)
        s("bg8", [8, 2], F32)
        s("epscol", [128, 1], F32)
        self.psum("PB0", [128, 512])
        self.psum("PB1", [128, 512])
        self.psum("PB2", [128, 512])
        self.psum("PB3", [128, 512])
        self.psum("PB4", [128, 512])
        self.psum("PB5", [128, 512])
        self.psum("PB6", [128, 512])
        self.psum("PTR", [128, 128], BF16)

    def load_small_tables(self, need_rope=False, need_l0dec=False, need_l1mask=False):
        b, t, d = self.b, self.t, self.d
        ht = host_tables()
        epsv = Builder.v(t["epscol"])
        epap = Builder.ap(epsv)
        b.op("dve", lambda e, a=epap: e.memset(a, LN_EPS), writes=[epsv], dbg="eps")
        for name, dt_, shape in [
            ("identb", BF16, [128, 128]), ("identf", F32, [128, 128]),
            ("ones_col", BF16, [128, 1]), ("ones_row", BF16, [1, 512]),
            ("onehotE", BF16, [8, 1024]), ("onehotQ", BF16, [8, 256]),
        ]:
            di = self.dram_in("c_" + name, shape, dt_)
            b.dma("sp", Builder.v(t[name]), Builder.v(di), dbg=f"ld {name}")
        if need_rope:
            for nm in ("ropecos", "ropesin"):
                self.dram_in("c_" + nm, [128, 1024], BF16)
            self.sbuf("Prot", [128, 128], BF16)
            di = self.dram_in("c_Prot", [128, 128], BF16)
            b.dma("sp", Builder.v(t["Prot"]), Builder.v(di), dbg="ld Prot")
        if need_l0dec:
            self.dram_in("c_diagT", [128, 2048], BF16)
            self.dram_in("c_qdec", [128, 2048], BF16)
            self.dram_in("c_kdeccol", [128, 16], F32)
            self.sbuf("kdeccol", [128, 16], F32)
            b.dma("sp", Builder.v(t["kdeccol"]), Builder.v(self.d["c_kdeccol"]), dbg="kdeccol")
        if need_l1mask:
            self.sbuf("trilT", [128, 128], BF16)
            di = self.dram_in("c_trilT", [128, 128], BF16)
            b.dma("sp", Builder.v(t["trilT"]), Builder.v(di), dbg="ld trilT")

    def mm(self, out_v, lhsT_v, rhs_v, start, stop, tp=None, dbg=""):
        oa, la, ra = Builder.ap(out_v), Builder.ap(lhsT_v), Builder.ap(rhs_v)
        if tp is None and lhsT_v[1] == 96:
            tp = (96, 0)
        if tp == (0, 0):
            tp = None
        self.b.op("pe", lambda e, oa=oa, la=la, ra=ra, s=start, p=stop, tp=tp:
                  e.matmul(oa, la, ra, start=s, stop=p, tile_position=tp, skip_group_check=True),
                  reads=[lhsT_v, rhs_v], writes=[out_v], dbg=dbg)

    def act(self, out_v, in_v, func, bias=0.0, scale=1.0, dbg=""):
        oa, ia = Builder.ap(out_v), Builder.ap(in_v)
        reads = [in_v]
        if isinstance(bias, tuple):
            reads.append(bias)
            bias_ap = Builder.ap(bias)
        else:
            bias_ap = bias
        self.b.op("act", lambda e, oa=oa, ia=ia, f=func, b_=bias_ap, s=scale:
                  e.activation(oa, ia, f, bias=b_, scale=s),
                  reads=reads, writes=[out_v], dbg=dbg)

    def tt(self, eng, out_v, in0_v, in1_v, op, dbg=""):
        oa, a0, a1 = Builder.ap(out_v), Builder.ap(in0_v), Builder.ap(in1_v)
        self.b.op(eng, lambda e, oa=oa, a0=a0, a1=a1, op=op: e.tensor_tensor(oa, a0, a1, op),
                  reads=[in0_v, in1_v], writes=[out_v], dbg=dbg)

    def ts(self, eng, out_v, in0_v, s1, s2, op0, op1=ALU.bypass, dbg=""):
        oa, a0 = Builder.ap(out_v), Builder.ap(in0_v)
        reads = [in0_v]
        if isinstance(s1, tuple):
            reads.append(s1)
            s1 = Builder.ap(s1)
        if isinstance(s2, tuple):
            reads.append(s2)
            s2 = Builder.ap(s2)
        self.b.op(eng, lambda e, oa=oa, a0=a0, s1=s1, s2=s2, op0=op0, op1=op1:
                  e.tensor_scalar(oa, a0, s1, s2, op0, op1),
                  reads=reads, writes=[out_v], dbg=dbg)

    def stt(self, eng, out_v, in0_v, scalar, in1_v, op0, op1, dbg=""):
        oa, a0, a1 = Builder.ap(out_v), Builder.ap(in0_v), Builder.ap(in1_v)
        reads = [in0_v, in1_v]
        if isinstance(scalar, tuple):
            reads.append(scalar)
            scalar = Builder.ap(scalar)
        self.b.op(eng, lambda e, oa=oa, a0=a0, sc=scalar, a1=a1, op0=op0, op1=op1:
                  e.scalar_tensor_tensor(oa, a0, sc, a1, op0, op1),
                  reads=reads, writes=[out_v], dbg=dbg)

    def red(self, out_v, in_v, op, negate=False, dbg=""):
        oa, ia = Builder.ap(out_v), Builder.ap(in_v)
        self.b.op("dve", lambda e, oa=oa, ia=ia, op=op, n=negate:
                  e.tensor_reduce(oa, ia, AX.X, op, negate=n),
                  reads=[in_v], writes=[out_v], dbg=dbg)

    def copy(self, eng, out_v, in_v, dbg=""):
        oa, ia = Builder.ap(out_v), Builder.ap(in_v)
        self.b.op(eng, lambda e, oa=oa, ia=ia: e.tensor_copy(oa, ia),
                  reads=[in_v], writes=[out_v], dbg=dbg)

    def recip(self, out_v, in_v, dbg=""):
        oa, ia = Builder.ap(out_v), Builder.ap(in_v)
        self.b.op("dve", lambda e, oa=oa, ia=ia: e.reciprocal(oa, ia),
                  reads=[in_v], writes=[out_v], dbg=dbg)

    def transpose(self, out_v, in_v, ident_name, dbg=""):
        oa, ia = Builder.ap(out_v), Builder.ap(in_v)
        np_ = in_v[2] - in_v[1]
        idv = Builder.v(self.t[ident_name], (0, np_), (0, np_))
        ida = Builder.ap(idv)
        self.b.op("pe", lambda e, oa=oa, ia=ia, ida=ida: e.transpose(oa, ia, ida),
                  reads=[in_v, idv], writes=[out_v], dbg=dbg)

    def stage_ln(self):
        b, t, P = self.b, self.t, self.p
        V = Builder.v
        xh = self.d["xh"]
        xln, xch, sq, gfm = t["ARBb"], t["XCH"], t["SQB"], t["ARD"]
        b.barrier()
        for dc in range(DC):
            slot = dc % 2
            xv = V(xch, (0, 128), (slot * 1024, slot * 1024 + 1024))
            b.dma("sp" if dc % 2 == 0 else "act", xv,
                  V(xh, (dc * 128, dc * 128 + 128), (0, TCORE)), dbg=f"ln ldx{dc}")
            xlv = V(xln, (0, 128), (dc * 1024, dc * 1024 + 1024))
            self.copy("dve", xlv, xv, dbg=f"ln cast{dc}")
            sqv = V(sq, (0, 128), (slot * 1024, slot * 1024 + 1024))
            self.act(sqv, xv, AF.Square, dbg=f"ln sq{dc}")
            for tt in range(NTT):
                st, sp = dc == 0, dc == DC - 1
                self.mm(V(P[f"PB{tt}"], (0, 1), (0, 512)), V(t["ones_col"]),
                        V(xln, (0, 128), (dc * 1024 + tt * 512, dc * 1024 + tt * 512 + 512)),
                        st, sp, dbg=f"ln summ{dc}.{tt}")
                self.mm(V(P[f"PB{2+tt}"], (0, 1), (0, 512)), V(t["ones_col"]),
                        V(sq, (0, 128), (slot * 1024 + tt * 512, slot * 1024 + tt * 512 + 512)),
                        st, sp, dbg=f"ln sqm{dc}.{tt}")
        for tt in range(NTT):
            m = V(gfm, (0, 1), (0, 512))
            var = V(gfm, (0, 1), (512, 1024))
            rstd = V(gfm, (0, 1), (1024, 1536))
            mr = V(gfm, (0, 1), (1536, 2048))
            self.ts("dve", m, V(P[f"PB{tt}"], (0, 1), (0, 512)), 1.0 / D, None, ALU.mult, dbg="ln m")
            self.tt("dve", var, m, m, ALU.mult, dbg="ln msq")
            self.stt("dve", var, V(P[f"PB{2+tt}"], (0, 1), (0, 512)), 1.0 / D, var,
                     ALU.mult, ALU.subtract, dbg="ln var")
            self.act(mr, var, AF.Sqrt, bias=V(self.t["epscol"], (0, 1), (0, 1)), dbg="ln sqrt")
            self.recip(rstd, mr, dbg="ln rstd")
            self.tt("dve", mr, m, rstd, ALU.mult, dbg="ln mr")
            rb = V(t["CFM"], (0, 1), (0, 512))
            mrb = V(t["CFM"], (0, 1), (512, 1024))
            self.copy("dve", rb, rstd, dbg="ln rbf")
            self.copy("dve", mrb, mr, dbg="ln mrbf")
            self.mm(V(P["PB4"], (0, 128), (0, 512)), V(t["ones_row"], (0, 1), (0, 128)), rb,
                    True, True, dbg="ln bcr")
            self.mm(V(P["PB5"], (0, 128), (0, 512)), V(t["ones_row"], (0, 1), (0, 128)), mrb,
                    True, True, dbg="ln bcmr")
            self.act(V(t["RSB"], (0, 128), (tt * 512, tt * 512 + 512)), V(P["PB4"], (0, 128), (0, 512)),
                     AF.Identity, dbg="ln evr")
            self.act(V(t["MRSB"], (0, 128), (tt * 512, tt * 512 + 512)), V(P["PB5"], (0, 128), (0, 512)),
                     AF.Identity, dbg="ln evmr")
        for dc in range(DC):
            for tt in range(NTT):
                xs = V(xln, (0, 128), (dc * 1024 + tt * 512, dc * 1024 + tt * 512 + 512))
                tmp = V(sq, (0, 128), (tt * 512, tt * 512 + 512))
                self.tt("dve", tmp, xs, V(t["RSB"], (0, 128), (tt * 512, tt * 512 + 512)),
                        ALU.mult, dbg=f"ln ap1 {dc}.{tt}")
                self.tt("dve", xs, tmp, V(t["MRSB"], (0, 128), (tt * 512, tt * 512 + 512)),
                        ALU.subtract, dbg=f"ln ap2 {dc}.{tt}")

    def gate_topk(self, wblk_view, bias8_view, mode, src_t=None):
        b, t, P = self.b, self.t, self.p
        V = Builder.v
        xln = src_t if src_t is not None else t["ARBb"]
        gfm, gtm, gsc, cfm = t["GFM"], t["GTM"], t["GSC"], t["CFM"]
        wt, wp0, wp1, wf0, wf1 = wblk_view
        for tt in range(NTT):
            for kc in range(DC):
                self.mm(V(P["PB0"], (0, 8), (0, 512)),
                        (wt, wp0, wp1, wf0 + kc * 8, wf0 + kc * 8 + 8),
                        V(xln, (0, 128), (kc * 1024 + tt * 512, kc * 1024 + tt * 512 + 512)),
                        kc == 0, kc == DC - 1, dbg=f"gate mm{tt}.{kc}")
            self.act(V(gfm, (0, 8), (tt * 512, tt * 512 + 512)), V(P["PB0"], (0, 8), (0, 512)),
                     AF.Identity, bias=bias8_view, dbg=f"gate ev{tt}")
        for tc in range(8):
            pb = P["PB1"] if tc % 2 == 0 else P["PB2"]
            self.transpose(V(pb, (0, 128), (0, 8)), V(gfm, (0, 8), (tc * 128, tc * 128 + 128)),
                           "identf", dbg=f"gate tr{tc}")
            self.copy("dve", V(gtm, (0, 128), (tc * 8, tc * 8 + 8)), V(pb, (0, 128), (0, 8)),
                      dbg=f"gate cp{tc}")
        xchf = t["XCH"]
        c0s = [V(gsc, (0, 128), (tc, tc + 1)) for tc in range(8)]
        c1s = [V(gsc, (0, 128), (8 + tc, 9 + tc)) for tc in range(8)]
        gs = [V(gtm, (0, 128), (tc * 8, tc * 8 + 8)) for tc in range(8)]
        pbufs = [V(xchf, (0, 128), (tc * 8, tc * 8 + 8)) for tc in range(8)]
        mbufs = [V(xchf, (0, 128), (512 + tc * 8, 512 + tc * 8 + 8)) for tc in range(8)]
        wbufs = [V(xchf, (0, 128), (1024 + tc * 8, 1024 + tc * 8 + 8)) for tc in range(8)]
        if mode == "l0":
            for tc in range(8):
                self.red(c0s[tc], gs[tc], ALU.max, negate=True, dbg=f"g0 nm{tc}")
            for tc in range(8):
                self.act(pbufs[tc], gs[tc], AF.Exp, bias=c0s[tc], dbg=f"g0 exp{tc}")
            for tc in range(8):
                self.red(c1s[tc], pbufs[tc], ALU.add, dbg=f"g0 sum{tc}")
            for tc in range(8):
                self.recip(c0s[tc], c1s[tc], dbg=f"g0 rec{tc}")
            for tc in range(8):
                self.ts("dve", pbufs[tc], pbufs[tc], c0s[tc], None, ALU.mult, dbg=f"g0 p{tc}")
            for tc in range(8):
                self.red(c0s[tc], pbufs[tc], ALU.max, negate=True, dbg=f"g0 nm1{tc}")
            for tc in range(8):
                self.ts("dve", mbufs[tc], pbufs[tc], c0s[tc], 0.0, ALU.add, ALU.is_equal, dbg=f"g0 eq1{tc}")
            for tc in range(8):
                self.stt("dve", mbufs[tc], mbufs[tc], -BIG, pbufs[tc], ALU.mult, ALU.add, dbg=f"g0 msk{tc}")
            for tc in range(8):
                self.red(c0s[tc], mbufs[tc], ALU.max, negate=True, dbg=f"g0 nm2{tc}")
            for tc in range(8):
                self.ts("dve", mbufs[tc], pbufs[tc], c0s[tc], 0.0, ALU.add, ALU.is_ge, dbg=f"g0 ge2{tc}")
            for tc in range(8):
                self.tt("dve", wbufs[tc], pbufs[tc], mbufs[tc], ALU.mult, dbg=f"g0 comb{tc}")
        else:
            for tc in range(8):
                self.red(c0s[tc], gs[tc], ALU.min, negate=True, dbg=f"g1 n1{tc}")
            for tc in range(8):
                self.ts("dve", mbufs[tc], gs[tc], c0s[tc], 0.0, ALU.add, ALU.is_equal, dbg=f"g1 eq1{tc}")
            for tc in range(8):
                self.stt("dve", mbufs[tc], mbufs[tc], BIG, gs[tc], ALU.mult, ALU.add, dbg=f"g1 msk{tc}")
            for tc in range(8):
                self.red(c0s[tc], mbufs[tc], ALU.min, negate=True, dbg=f"g1 n2{tc}")
            for tc in range(8):
                self.ts("dve", mbufs[tc], gs[tc], c0s[tc], 0.0, ALU.is_gt if False else ALU.add, ALU.is_gt, dbg=f"g1 top{tc}")
            for tc in range(8):
                self.red(c1s[tc], gs[tc], ALU.max, negate=True, dbg=f"g1 nmx{tc}")
            for tc in range(8):
                self.act(pbufs[tc], gs[tc], AF.Exp, bias=c1s[tc], dbg=f"g1 exp{tc}")
            for tc in range(8):
                self.tt("dve", pbufs[tc], pbufs[tc], mbufs[tc], ALU.mult, dbg=f"g1 em{tc}")
            for tc in range(8):
                self.red(c1s[tc], pbufs[tc], ALU.add, dbg=f"g1 sum{tc}")
            for tc in range(8):
                self.recip(c0s[tc], c1s[tc], dbg=f"g1 rec{tc}")
            for tc in range(8):
                self.ts("dve", wbufs[tc], pbufs[tc], c0s[tc], None, ALU.mult, dbg=f"g1 w{tc}")
        for tc in range(8):
            pb = P["PB1"] if tc % 2 == 0 else P["PB2"]
            self.transpose(V(pb, (0, 8), (0, 128)), wbufs[tc], "identf", dbg=f"g tr{tc}")
            self.act(V(cfm, (0, 8), (tc * 128, tc * 128 + 128)), V(pb, (0, 8), (0, 128)),
                     AF.Identity, dbg=f"g ev{tc}")
        if mode == "l0":
            for tt in range(NTT):
                self.mm(V(P["PB0"], (0, 1), (0, 512)), V(t["ones_col"], (0, 8), (0, 1)),
                        V(cfm, (0, 8), (tt * 512, tt * 512 + 512)), True, True, dbg=f"csum mm{tt}")
                self.copy("dve", V(t["RSB"], (0, 1), (tt * 512, tt * 512 + 512)),
                          V(P["PB0"], (0, 1), (0, 512)), dbg=f"csum ev{tt}")

    def stage_l0ffn(self):
        b, t, P, d = self.b, self.t, self.p, self.d
        V = Builder.v
        xln = t["ARBb"]
        arc = t["ARCb"]
        ara = t["ARAb"]
        sq = t["SQB"]
        b.barrier()
        b.dma("pool", V(t["SV0"], None, (0, 128)), V(d["l0_wr"], (0, 128), (0, 128)), dbg="l0f wr")
        b.dma("pool", V(t["browbuf"], (0, 1), (0, 2048)), V(d["l0_b2row"], (0, 1), (0, 2048)), dbg="l0f b2row")
        for q in range(2):
            b.dma("pool", V(arc, None, (6656 + q * 2048, 6656 + (q + 1) * 2048)),
                  V(d["l0_b2q"], (q * 128, (q + 1) * 128), (0, 2048)), dbg=f"l0f b2q{q}")
        self.gate_topk(V(t["SV0"], (0, 128), (0, 128)), V(t["br8"], (0, 8), (0, 1)), "l0")
        b.dma("pool", V(t["SV0"], None, (0, 2048)), V(d["l0_a1"], (0, 128), (0, 2048)), dbg="l0f a1a")
        b.dma("pool", V(t["SV1"], None, (0, 2048)), V(d["l0_a1"], (128, 256), (0, 2048)), dbg="l0f a1b")
        for mh in range(2):
            svt = t["SV0"] if mh == 0 else t["SV1"]
            for tt in range(NTT):
                pb = P["PB1"] if mh == 0 else P["PB2"]
                for kc in range(DC):
                    self.mm(V(pb, (0, 128), (0, 512)), V(svt, (0, 128), (kc * 128, kc * 128 + 128)),
                            V(xln, (0, 128), (kc * 1024 + tt * 512, kc * 1024 + tt * 512 + 512)),
                            kc == 0, kc == DC - 1, dbg=f"t1 mm{mh}.{tt}.{kc}")
                self.act(V(arc, (0, 128), (4096 + mh * 1024 + tt * 512, 4096 + mh * 1024 + tt * 512 + 512)),
                         V(pb, (0, 128), (0, 512)), AF.Identity, scale=2.0, dbg=f"t1 ev{mh}.{tt}")
        for tt in range(NTT):
            for e in range(E):
                self.mm(V(P["PB3"], (0, 128), (0, 512)), V(t["onehotE"], (0, 8), (e * 128, e * 128 + 128)),
                        V(t["CFM"], (0, 8), (tt * 512, tt * 512 + 512)), True, True, dbg=f"cb mm{e}")
                self.act(V(arc, (0, 128), (e * 512, e * 512 + 512)), V(P["PB3"], (0, 128), (0, 512)),
                         AF.Identity, dbg=f"cb ev{e}")
            for fc in range(FC):
                sv = t["SV0"] if fc % 2 == 0 else t["SV1"]
                b.dma("pool", V(sv, None, (0, 2560)),
                      V(d["l0_w1blk"], (fc * 128, (fc + 1) * 128), (0, 2560)), dbg=f"w1 ld{fc}")
                for kc in range(DC):
                    self.mm(V(P["PB0"], (0, 128), (0, 512)), V(sv, (0, 128), (kc * 128, kc * 128 + 128)),
                            V(xln, (0, 128), (kc * 1024 + tt * 512, kc * 1024 + tt * 512 + 512)),
                            kc == 0, kc == DC - 1, dbg=f"base mm{fc}.{kc}")
                basev = V(arc, (0, 128), (6144, 6656)) if fc % 2 == 0 else V(arc, (0, 128), (11776, 12288))
                self.act(basev, V(P["PB0"], (0, 128), (0, 512)), AF.Identity,
                         bias=V(t["biasfm"], (0, 128), (fc, fc + 1)), dbg=f"base ev{fc}")

                def flush(e, aev):
                    cbv = V(arc, (0, 128), (e * 512, e * 512 + 512))
                    mixv = V(ara, (0, 128), (fc * 512, fc * 512 + 512))
                    if e == 0:
                        self.tt("dve", mixv, aev, cbv, ALU.mult, dbg=f"mix0 {fc}")
                    else:
                        cav = V(sq, (0, 128), (1024 + (e % 2) * 512, 1536 + (e % 2) * 512))
                        self.tt("dve", cav, aev, cbv, ALU.mult, dbg=f"ca {fc}.{e}")
                        self.tt("dve", mixv, mixv, cav, ALU.add, dbg=f"mixa {fc}.{e}")
                    q = e // 4
                    r = e % 4
                    pt = P["PB5"] if q == 0 else P["PB6"]
                    self.mm((pt, 32 * r, 32 * r + 32, 0, 512),
                            (sv, 0, 128, 2048 + e * 32, 2048 + e * 32 + 32),
                            aev, fc == 0, fc == FC - 1,
                            tp=(0, 32 * r), dbg=f"t2 mm{fc}.{e}")

                pend = None
                for e in range(E):
                    pb = P["PB1"] if e % 2 == 0 else P["PB2"]
                    q = e // 4
                    r = e % 4
                    self.mm(V(pb, (0, 128), (0, 512)),
                            (sv, 32 * r, 32 * r + 32, 2304 + q * 128, 2304 + (q + 1) * 128),
                            (arc, 32 * r, 32 * r + 32, 4096 + q * 1024 + tt * 512, 4096 + q * 1024 + tt * 512 + 512),
                            True, False, dbg=f"l1 mm{fc}.{e}")
                    self.mm(V(pb, (0, 128), (0, 512)), V(t["identb"]), basev,
                            False, True, dbg=f"l1 id{fc}.{e}")
                    if pend is not None:
                        flush(*pend)
                    aev = V(sq, (0, 128), ((e % 2) * 512, (e % 2) * 512 + 512))
                    self.act(aev, V(pb, (0, 128), (0, 512)), AF.Gelu_apprx_tanh, dbg=f"gelu{fc}.{e}")
                    pend = (e, aev)
                flush(*pend)
            for q in range(2):
                pt = P["PB5"] if q == 0 else P["PB6"]
                t2x = V(sq, (0, 128), (1024, 1536))
                self.act(t2x, V(pt, (0, 128), (0, 512)), AF.Identity, scale=2.0, dbg=f"t2x {q}")
                self.mm(V(P["PB3"], (0, 128), (0, 512)), V(t["onehotQ"], (0, 8), (q * 128, q * 128 + 128)),
                        V(t["CFM"], (0, 8), (tt * 512, tt * 512 + 512)), True, True, dbg=f"cbq mm{q}")
                cbq = V(sq, (0, 128), (1536, 2048))
                self.act(cbq, V(P["PB3"], (0, 128), (0, 512)), AF.Identity, dbg=f"cbq ev{q}")
                self.tt("dve", V(arc, (0, 128), (10752 + q * 512, 10752 + (q + 1) * 512)),
                        t2x, cbq, ALU.mult, dbg=f"t2c {q}")
            for dc in range(DC):
                po = P["PB3"] if dc % 2 == 0 else P["PB4"]
                pov = V(po, (0, 128), (0, 512))
                for qtr in range(4):
                    swt = t["SW0"] if qtr % 2 == 0 else t["SW1"]
                    b.dma("pool", V(swt, None, (0, 2048)),
                          V(d["l0_w2blk"], (dc * 128, (dc + 1) * 128), (qtr * 2048, (qtr + 1) * 2048)),
                          dbg=f"w2 ld{dc}.{qtr}")
                    for k in range(16):
                        kc = qtr * 16 + k
                        self.mm(pov, V(swt, (0, 128), (k * 128, k * 128 + 128)),
                                V(ara, (0, 128), (kc * 512, kc * 512 + 512)),
                                kc == 0, False, dbg=f"w2 mm{dc}.{kc}")
                self.mm(pov, V(t["browbuf"], (0, 1), (dc * 128, dc * 128 + 128)),
                        V(t["RSB"], (0, 1), (tt * 512, tt * 512 + 512)), False, False, dbg=f"b2 mm{dc}")
                for q in range(2):
                    self.mm(pov, V(arc, (0, 128), (6656 + q * 2048 + dc * 128, 6656 + q * 2048 + (dc + 1) * 128)),
                            V(arc, (0, 128), (10752 + q * 512, 10752 + (q + 1) * 512)),
                            False, q == 1, dbg=f"l2 mm{dc}.{q}")
                xslot = (dc % 2) * 1024 + tt * 512
                xv = V(t["XCH"], (0, 128), (xslot, xslot + 512))
                b.dma("sp", xv, V(d["xh"], (dc * 128, (dc + 1) * 128), (tt * 512, tt * 512 + 512)),
                      dbg=f"res ldx{dc}.{tt}")
                self.stt("dve", xv, pov, 0.5, xv, ALU.mult, ALU.add, dbg=f"res {dc}.{tt}")
                b.dma("sp", V(d["xh"], (dc * 128, (dc + 1) * 128), (tt * 512, tt * 512 + 512)), xv,
                      dbg=f"res stx{dc}.{tt}")

    def attn0_projections(self):
        b, t, P, d = self.b, self.t, self.p, self.d
        V = Builder.v
        xln, ara, arc, xch, sq = t["ARBb"], t["ARAb"], t["ARCb"], t["XCH"], t["SQB"]
        b.barrier()
        b.dma("sp", V(arc, (0, 128), (12288, 13312)), V(d["c_ropecos"]), dbg="ropec")
        b.dma("sp", V(arc, (0, 128), (14336, 15360)), V(d["c_ropesin"]), dbg="ropes")
        vbufs = [t["SW0"], t["SW1"], t["SV0"], t["SV1"]]
        for qt in range(4):
            b.dma("pool", V(vbufs[qt], None, (0, 2048)),
                  V(d["l0_wvT"], (0, 128), (qt * 2048, (qt + 1) * 2048)), dbg=f"wv{qt}")
        for tc in range(8):
            pb = P["PB4"] if tc % 2 == 0 else P["PB5"]
            for kc in range(DC):
                sw = vbufs[kc // 4]
                self.mm(V(pb, (0, 128), (0, 512)),
                        V(xln, (0, 128), (kc * 1024 + tc * 128, kc * 1024 + tc * 128 + 128)),
                        V(sw, (0, 128), ((kc % 4) * 512, (kc % 4) * 512 + 512)),
                        kc == 0, kc == DC - 1, dbg=f"v mm{tc}.{kc}")
            self.act(V(arc, (0, 128), (4096 + tc * 512, 4096 + (tc + 1) * 512)),
                     V(pb, (0, 128), (0, 512)), AF.Identity, dbg=f"v ev{tc}")
        self.proj_rope_heads(H, "l0_wqT", ara, 0, elu=False)
        self.proj_rope_heads(GKV, "l0_wkT", arc, 0, elu=False)
        self.k_to_tm()

    def k_to_tm(self):
        t, P = self.t, self.p
        V = Builder.v
        arc = t["ARCb"]
        for tc in range(8):
            for g in range(GKV):
                self.transpose(V(P["PTR"], (0, 128), (0, 128)),
                               V(arc, (0, 128), (g * 1024 + tc * 128, g * 1024 + (tc + 1) * 128)),
                               "identb", dbg=f"ktm t{tc}.{g}")
                self.copy("dve", V(arc, (0, 128), (8192 + tc * 512 + g * 128, 8192 + tc * 512 + (g + 1) * 128)),
                          V(P["PTR"], (0, 128), (0, 128)), dbg=f"ktme t{tc}.{g}")

    def attn0_scan(self, with_outputs):
        b, t, P = self.b, self.t, self.p
        V = Builder.v
        ara, arc = t["ARAb"], t["ARCb"]
        ard, ardb = t["ARD"], t["ARDb"]
        sq = t["SQB"]
        ht = host_tables()
        blk = ht["blk"]
        b.dma("sp", V(arc, (0, 128), (12288, 14336)), V(self.d["c_diagT"]), dbg="diagT")
        b.dma("sp", V(arc, (0, 128), (14336, 16384)), V(self.d["c_qdec"]), dbg="qdec")
        obank = [P["PB2"], P["PB4"], P["PB5"], P["PB6"]]
        for c in range(NCH0):
            if with_outputs:
                for h in range(H):
                    self.act(V(ardb, (0, 128), (4096 + h * 128, 4096 + (h + 1) * 128)),
                             V(ard, (0, 128), (h * 128, (h + 1) * 128)), AF.Identity, dbg=f"kvb{c}.{h}")
                for h in range(H):
                    g = h // 4
                    pb_sc = P["PB1"] if h % 2 == 0 else P["PB3"]
                    self.mm(V(pb_sc, (0, 128), (0, 128)),
                            V(arc, (0, 128), (g * 1024 + c * 128, g * 1024 + (c + 1) * 128)),
                            V(ara, (0, 128), (h * 1024 + c * 128, h * 1024 + (c + 1) * 128)),
                            True, True, dbg=f"sc{c}.{h}")
                    self.tt("dve", V(ardb, (0, 128), (6144 + h * 128, 6144 + (h + 1) * 128)),
                            V(pb_sc, (0, 128), (0, 128)),
                            V(arc, (0, 128), (12288 + h * 128, 12288 + (h + 1) * 128)),
                            ALU.mult, dbg=f"sm{c}.{h}")
                    self.tt("pool", V(sq, (0, 128), (h * 128, (h + 1) * 128)),
                            V(ara, (0, 128), (h * 1024 + c * 128, h * 1024 + (c + 1) * 128)),
                            V(arc, (0, 128), (14336 + h * 128, 14336 + (h + 1) * 128)),
                            ALU.mult, dbg=f"qd{c}.{h}")
                for h in range(H):
                    g = h // 4
                    ob = obank[h // 4]
                    col = (h % 4) * 128
                    self.mm(V(ob, (0, 128), (col, col + 128)),
                            V(arc, (0, 128), (4096 + c * 512 + g * 128, 4096 + c * 512 + (g + 1) * 128)),
                            V(ardb, (0, 128), (6144 + h * 128, 6144 + (h + 1) * 128)),
                            True, False, dbg=f"oi{c}.{h}")
                    self.mm(V(ob, (0, 128), (col, col + 128)),
                            V(ardb, (0, 128), (4096 + h * 128, 4096 + (h + 1) * 128)),
                            V(sq, (0, 128), (h * 128, (h + 1) * 128)),
                            False, True, dbg=f"ox{c}.{h}")
                for h in range(H):
                    ob = obank[h // 4]
                    col = (h % 4) * 128
                    self.act(V(ara, (0, 128), (16384 + h * 1024 + c * 128, 16384 + h * 1024 + (c + 1) * 128)),
                             V(ob, (0, 128), (col, col + 128)), AF.Identity, dbg=f"oev{c}.{h}")
            for h in range(H):
                g = h // 4
                self.ts("pool", V(sq, (0, 128), (2048 + h * 128, 2048 + (h + 1) * 128)),
                        V(arc, (0, 128), (8192 + c * 512 + g * 128, 8192 + c * 512 + (g + 1) * 128)),
                        V(t["kdeccol"], (0, 128), (h, h + 1)), None, ALU.mult, dbg=f"kd{c}.{h}")
            for h in range(H):
                g = h // 4
                ob = obank[h // 4]
                col = (h % 4) * 128
                self.mm(V(ob, (0, 128), (col, col + 128)),
                        V(sq, (0, 128), (2048 + h * 128, 2048 + (h + 1) * 128)),
                        V(arc, (0, 128), (4096 + c * 512 + g * 128, 4096 + c * 512 + (g + 1) * 128)),
                        True, True, dbg=f"dl{c}.{h}")
            for h in range(H):
                ob = obank[h // 4]
                col = (h % 4) * 128
                kvh = V(ard, (0, 128), (h * 128, (h + 1) * 128))
                self.stt("dve", kvh, kvh, float(blk[h]), V(ob, (0, 128), (col, col + 128)),
                         ALU.mult, ALU.add, dbg=f"kvu{c}.{h}")

    def stage_attn0_part1(self):
        b, t = self.b, self.t
        V = Builder.v
        self.attn0_projections()
        ardv = V(t["ARD"], (0, 128), (0, 2048))
        ap = Builder.ap(ardv)
        b.op("dve", lambda e, a=ap: e.memset(a, 0.0), writes=[ardv], dbg="kv0")
        self.attn0_scan(with_outputs=False)
        b.barrier()
        b.dma("sp", V(self.d["D0o"]), V(t["ARD"], (0, 128), (0, 2048)), dbg="D0 out")
        b.dma("sp", V(self.d["q0o"]), V(t["ARAb"], (0, 128), (0, 16384)), dbg="q0 out")
        b.dma("sp", V(self.d["k0o"]), V(t["ARCb"], (0, 128), (0, 4096)), dbg="k0 out")
        b.dma("sp", V(self.d["v0o"]), V(t["ARCb"], (0, 128), (4096, 8192)), dbg="v0 out")

    def stage_attn0_part2(self):
        b, t, P, d = self.b, self.t, self.p, self.d
        V = Builder.v
        b.barrier()
        b.dma("sp", V(t["ARAb"], (0, 128), (0, 16384)), V(d["q0i"]), dbg="q0 in")
        b.dma("sp", V(t["ARCb"], (0, 128), (0, 4096)), V(d["k0i"]), dbg="k0 in")
        b.dma("sp", V(t["ARCb"], (0, 128), (4096, 8192)), V(d["v0i"]), dbg="v0 in")
        self.sbuf("coefL0", [128, 128], F32)
        ci = self.dram_in("c_coefL0", [128, 128], F32)
        b.dma("sp", V(t["coefL0"]), V(ci), dbg="coefL0")
        xch = t["XCH"]
        for j in range(NCORES):
            xv = V(xch, (0, 128), (0, 2048))
            b.dma("sp", xv, V(d["D0all"], (j * 128, (j + 1) * 128), (0, 2048)), dbg=f"D0 {j}")
            for h in range(H):
                kvh = V(t["ARD"], (0, 128), (h * 128, (h + 1) * 128))
                dsl = V(xch, (0, 128), (h * 128, (h + 1) * 128))
                cc = V(t["coefL0"], (0, 128), (j * 16 + h, j * 16 + h + 1))
                if j == 0:
                    self.ts("dve", kvh, dsl, cc, None, ALU.mult, dbg=f"seed{j}.{h}")
                else:
                    self.stt("dve", kvh, dsl, cc, kvh, ALU.mult, ALU.add, dbg=f"seed{j}.{h}")
        self.k_to_tm()
        self.attn0_scan(with_outputs=True)
        b.barrier()
        for dc in range(DC):
            sw = t["SW0"] if dc % 2 == 0 else t["SW1"]
            b.dma("pool", V(sw), V(d["l0_woT"], (dc * 128, (dc + 1) * 128), (0, 2048)), dbg=f"wo{dc}")
            xslot = (dc % 2) * 1024
            xv = V(t["XCH"], (0, 128), (xslot, xslot + 1024))
            b.dma("sp", xv, V(d["xh"], (dc * 128, (dc + 1) * 128), (0, TCORE)), dbg=f"ares ld{dc}")
            for tt in range(NTT):
                po = P["PB3"] if tt == 0 else P["PB4"]
                pov = V(po, (0, 128), (0, 512))
                for kc in range(DC):
                    self.mm(pov, V(sw, (0, 128), (kc * 128, (kc + 1) * 128)),
                            V(t["ARAb"], (0, 128), (16384 + kc * 1024 + tt * 512, 16384 + kc * 1024 + tt * 512 + 512)),
                            kc == 0, kc == DC - 1, dbg=f"wo mm{dc}.{tt}.{kc}")
                xvt = V(t["XCH"], (0, 128), (xslot + tt * 512, xslot + tt * 512 + 512))
                self.tt("dve", xvt, xvt, pov, ALU.add, dbg=f"ares {dc}.{tt}")
            b.dma("sp", V(d["xh"], (dc * 128, (dc + 1) * 128), (0, TCORE)), xv, dbg=f"ares st{dc}")

    def stage_l1ffn(self):
        b, t, P, d = self.b, self.t, self.p, self.d
        V = Builder.v
        res = t["ARBb"]
        xt1 = t["ARCb"]
        oa = t["ARA"]
        hG = t["ARDb"]
        sG = t["SQB"]
        for r in range(2):
            xt = res if r == 0 else xt1
            b.barrier()
            b.dma("pool", V(t["SV0"], None, (0, 128)), V(d["l1_wg"], (r * 128, (r + 1) * 128), (0, 128)),
                  dbg=f"wg{r}")
            self.gate_topk(V(t["SV0"], (0, 128), (0, 128)), V(t["bg8"], (0, 8), (r, r + 1)), "l1", src_t=xt)
            b.dma("pool", V(t["browbuf"], (0, 1), (0, 2048)), V(d["l1_bsdrow"]), dbg=f"bsd{r}")
            for s in range(2):
                for mc in range(8):
                    sw = [t["SW0"], t["SW1"], t["SV0"], t["SV1"]][mc % 4]
                    b.dma("pool", V(sw, None, (0, 2048)), V(d["l1_wsu"], ((s * 8 + mc) * 128, (s * 8 + mc + 1) * 128), (0, 2048)),
                          dbg=f"wsu{s}.{mc}")
                    for tt in range(NTT):
                        pb = P["PB0"] if tt == 0 else P["PB1"]
                        for kc in range(DC):
                            self.mm(V(pb, (0, 128), (0, 512)), V(sw, (0, 128), (kc * 128, (kc + 1) * 128)),
                                    V(xt, (0, 128), (kc * 1024 + tt * 512, kc * 1024 + tt * 512 + 512)),
                                    kc == 0, kc == DC - 1, dbg=f"sh mm{s}.{mc}.{tt}")
                        self.act(V(hG, (0, 128), (mc * 1024 + tt * 512, mc * 1024 + tt * 512 + 512)),
                                 V(pb, (0, 128), (0, 512)), AF.Identity,
                                 bias=V(t["biasfm"], (0, 128), (160 + s * 8 + mc, 160 + s * 8 + mc + 1)),
                                 dbg=f"sh ev{s}.{mc}.{tt}")
                for tt in range(NTT):
                    for mc2 in range(8):
                        sv = [t["SW0"], t["SW1"], t["SV0"], t["SV1"]][mc2 % 4]
                        b.dma("pool", V(sv, None, (0, 1024)),
                              V(d["l1_wsa"], (mc2 * 128, (mc2 + 1) * 128), (0, 1024)), dbg=f"wsa{mc2}")
                        pg = P["PB2"] if mc2 % 2 == 0 else P["PB5"]
                        for kc in range(8):
                            self.mm(V(pg, (0, 128), (0, 512)), V(sv, (0, 128), (kc * 128, (kc + 1) * 128)),
                                    V(hG, (0, 128), (kc * 1024 + tt * 512, kc * 1024 + tt * 512 + 512)),
                                    kc == 0, kc == 7, dbg=f"sa mm{s}.{mc2}.{tt}")
                        self.act(V(sG, (0, 128), (mc2 * 512, (mc2 + 1) * 512)), V(pg, (0, 128), (0, 512)),
                                 AF.Silu, bias=V(t["biasfm"], (0, 128), (176 + mc2, 177 + mc2)),
                                 dbg=f"sa ev{s}.{mc2}.{tt}")
                    for mc in range(8):
                        hv = V(hG, (0, 128), (mc * 1024 + tt * 512, mc * 1024 + tt * 512 + 512))
                        self.tt("dve", hv, V(sG, (0, 128), (mc * 512, (mc + 1) * 512)), hv, ALU.mult,
                                dbg=f"sh h2{s}.{mc}.{tt}")
                for dc in range(DC):
                    sw = [t["SW0"], t["SW1"], t["SV0"], t["SV1"]][dc % 4]
                    b.dma("pool", V(sw, None, (0, 2048)), V(d["l1_wsd"], (dc * 128, (dc + 1) * 128), (0, 2048)), dbg=f"wsd{s}.{dc}")
                    for tt in range(NTT):
                        pb = P["PB3"] if tt == 0 else P["PB4"]
                        pov = V(pb, (0, 128), (0, 512))
                        for k8 in range(8):
                            kc = s * 8 + k8
                            self.mm(pov, V(sw, (0, 128), (kc * 128, (kc + 1) * 128)),
                                    V(hG, (0, 128), (k8 * 1024 + tt * 512, k8 * 1024 + tt * 512 + 512)),
                                    k8 == 0, s == 1 and k8 == 7, dbg=f"sd mm{s}.{dc}.{tt}.{k8}")
                        if s == 0:
                            self.mm(pov, V(t["browbuf"], (0, 1), (dc * 128, (dc + 1) * 128)),
                                    V(t["ones_row"], (0, 1), (0, 512)), False, True, dbg=f"bsd mm{dc}.{tt}")
                            self.act(V(oa, (0, 128), (dc * 1024 + tt * 512, dc * 1024 + tt * 512 + 512)),
                                     pov, AF.Identity, dbg=f"oa init{dc}.{tt}")
                        else:
                            oav = V(oa, (0, 128), (dc * 1024 + tt * 512, dc * 1024 + tt * 512 + 512))
                            self.tt("dve", oav, oav, pov, ALU.add, dbg=f"oa add{dc}.{tt}")
            for n in range(NEX):
                for tt in range(NTT):
                    self.mm(V(P["PB5"], (0, 128), (0, 512)), V(t["onehotE"], (0, 8), (n * 128, (n + 1) * 128)),
                            V(t["CFM"], (0, 8), (tt * 512, tt * 512 + 512)), True, True, dbg=f"wb mm{n}.{tt}")
                    self.act(V(t["RSB"], (0, 128), (tt * 512, tt * 512 + 512)), V(P["PB5"], (0, 128), (0, 512)),
                             AF.Identity, dbg=f"wb ev{n}.{tt}")
                b.dma("pool", V(t["bdnbuf"]), V(d["l1_bdnD"], (n * 8, (n + 1) * 8), (0, 2048)), dbg=f"bdn{n}")
                for mc in range(8):
                    sw = [t["SW0"], t["SW1"], t["SV0"], t["SV1"]][mc % 4]
                    b.dma("pool", V(sw, None, (0, 2048)), V(d["l1_wup"], ((n * 8 + mc) * 128, (n * 8 + mc + 1) * 128), (0, 2048)),
                          dbg=f"wup{n}.{mc}")
                    for tt in range(NTT):
                        pb = P["PB0"] if tt == 0 else P["PB1"]
                        for kc in range(DC):
                            self.mm(V(pb, (0, 128), (0, 512)), V(sw, (0, 128), (kc * 128, (kc + 1) * 128)),
                                    V(xt, (0, 128), (kc * 1024 + tt * 512, kc * 1024 + tt * 512 + 512)),
                                    kc == 0, kc == DC - 1, dbg=f"up mm{n}.{mc}.{tt}")
                        self.act(V(hG, (0, 128), (mc * 1024 + tt * 512, mc * 1024 + tt * 512 + 512)),
                                 V(pb, (0, 128), (0, 512)), AF.Identity,
                                 bias=V(t["biasfm"], (0, 128), (64 + n * 8 + mc, 64 + n * 8 + mc + 1)),
                                 dbg=f"up ev{n}.{mc}.{tt}")
                for tt in range(NTT):
                    for mc2 in range(8):
                        sv = [t["SW0"], t["SW1"], t["SV0"], t["SV1"]][mc2 % 4]
                        b.dma("pool", V(sv, None, (0, 1024)),
                              V(d["l1_wsw"], ((n * 8 + mc2) * 128, (n * 8 + mc2 + 1) * 128), (0, 1024)),
                              dbg=f"wsw{n}.{mc2}.{tt}")
                        pg = P["PB2"] if mc2 % 2 == 0 else P["PB5"]
                        for kc in range(8):
                            self.mm(V(pg, (0, 128), (0, 512)), V(sv, (0, 128), (kc * 128, (kc + 1) * 128)),
                                    V(hG, (0, 128), (kc * 1024 + tt * 512, kc * 1024 + tt * 512 + 512)),
                                    kc == 0, kc == 7, dbg=f"sw mm{n}.{mc2}.{tt}")
                        self.act(V(sG, (0, 128), (mc2 * 512, (mc2 + 1) * 512)), V(pg, (0, 128), (0, 512)),
                                 AF.Silu, bias=V(t["biasfm"], (0, 128), (112 + n * 8 + mc2, 112 + n * 8 + mc2 + 1)),
                                 dbg=f"sw ev{n}.{mc2}.{tt}")
                    for mc in range(8):
                        hv = V(hG, (0, 128), (mc * 1024 + tt * 512, mc * 1024 + tt * 512 + 512))
                        self.tt("dve", hv, V(sG, (0, 128), (mc * 512, (mc + 1) * 512)), hv, ALU.mult,
                                dbg=f"h2 {n}.{mc}.{tt}")
                        self.tt("dve", hv, hv, V(t["RSB"], (0, 128), (tt * 512, tt * 512 + 512)), ALU.mult,
                                dbg=f"h2w {n}.{mc}.{tt}")
                for dc in range(DC):
                    sv = [t["SW0"], t["SW1"], t["SV0"], t["SV1"]][dc % 4]
                    b.dma("pool", V(sv, None, (0, 1024)),
                          V(d["l1_wdn"], ((n * 16 + dc) * 128, (n * 16 + dc + 1) * 128), (0, 1024)),
                          dbg=f"wdn{n}.{dc}")
                    for tt in range(NTT):
                        pb = P["PB3"] if tt == 0 else P["PB4"]
                        pov = V(pb, (0, 128), (0, 512))
                        for kc in range(8):
                            self.mm(pov, V(sv, (0, 128), (kc * 128, (kc + 1) * 128)),
                                    V(hG, (0, 128), (kc * 1024 + tt * 512, kc * 1024 + tt * 512 + 512)),
                                    kc == 0, False, dbg=f"dn mm{n}.{dc}.{tt}.{kc}")
                        self.mm(pov, V(t["bdnbuf"], (0, 8), (dc * 128, (dc + 1) * 128)),
                                V(t["CFM"], (0, 8), (tt * 512, tt * 512 + 512)), False, True,
                                dbg=f"bdn mm{n}.{dc}.{tt}")
                        oav = V(oa, (0, 128), (dc * 1024 + tt * 512, dc * 1024 + tt * 512 + 512))
                        self.tt("dve", oav, oav, pov, ALU.add, dbg=f"oa a{n}.{dc}.{tt}")
            b.barrier()
            if r == 0:
                for dc in range(DC):
                    for tt in range(NTT):
                        c0 = dc * 1024 + tt * 512
                        self.tt("dve", V(xt1, (0, 128), (c0, c0 + 512)),
                                V(oa, (0, 128), (c0, c0 + 512)), V(res, (0, 128), (c0, c0 + 512)),
                                ALU.add, dbg=f"xt1 {dc}.{tt}")
            else:
                for dc in range(DC):
                    xslot = (dc % 2) * 1024
                    xv = V(t["XCH"], (0, 128), (xslot, xslot + 1024))
                    b.dma("sp", xv, V(d["xh"], (dc * 128, (dc + 1) * 128), (0, TCORE)), dbg=f"fres ld{dc}")
                    for tt in range(NTT):
                        c0 = dc * 1024 + tt * 512
                        u = V(t["ARD"], (0, 128), (2048 + tt * 512, 2048 + tt * 512 + 512))
                        self.tt("dve", u, V(oa, (0, 128), (c0, c0 + 512)), V(res, (0, 128), (c0, c0 + 512)),
                                ALU.add, dbg=f"fres u{dc}.{tt}")
                        xvt = V(t["XCH"], (0, 128), (xslot + tt * 512, xslot + tt * 512 + 512))
                        self.stt("dve", xvt, u, 0.5, xvt, ALU.mult, ALU.add, dbg=f"fres {dc}.{tt}")
                    b.dma("sp", V(d["xh"], (dc * 128, (dc + 1) * 128), (0, TCORE)), xv, dbg=f"fres st{dc}")

    def proj_rope_heads(self, nh, wname, dst, dstoff, elu):
        b, t, P, d = self.b, self.t, self.p, self.d
        V = Builder.v
        xln, xch, sq = t["ARBb"], t["XCH"], t["SQB"]
        for h in range(nh):
            sw = t["SW0"] if h % 2 == 0 else t["SW1"]
            b.dma("pool", V(sw), V(d[wname], (h * 128, (h + 1) * 128), (0, 2048)), dbg=f"{wname}{h}")
            for tt in range(NTT):
                pb = P["PB0"] if tt == 0 else P["PB1"]
                for kc in range(DC):
                    self.mm(V(pb, (0, 128), (0, 512)), V(sw, (0, 128), (kc * 128, (kc + 1) * 128)),
                            V(xln, (0, 128), (kc * 1024 + tt * 512, kc * 1024 + tt * 512 + 512)),
                            kc == 0, kc == DC - 1, dbg=f"qk mm{h}.{tt}.{kc}")
                qraw = V(sq, (0, 128), ((tt % 2) * 512, (tt % 2) * 512 + 512))
                self.act(qraw, V(pb, (0, 128), (0, 512)), AF.Identity, dbg=f"qraw{h}.{tt}")
                prot = P["PB2"] if tt == 0 else P["PB3"]
                self.mm(V(prot, (0, 128), (0, 512)), V(t["Prot"]), qraw, True, True, dbg=f"rot{h}.{tt}")
                s1 = V(xch, (0, 128), (0, 512))
                u = V(xch, (0, 128), (512, 1024))
                self.tt("dve", s1, qraw, V(t["ARCb"], (0, 128), (12288 + tt * 512, 12288 + tt * 512 + 512)),
                        ALU.mult, dbg=f"rope1 {h}.{tt}")
                self.tt("dve", u, V(prot, (0, 128), (0, 512)),
                        V(t["ARCb"], (0, 128), (14336 + tt * 512, 14336 + tt * 512 + 512)), ALU.mult, dbg=f"rope2 {h}.{tt}")
                dv = V(dst, (0, 128), (dstoff + h * 1024 + tt * 512, dstoff + h * 1024 + tt * 512 + 512))
                if not elu:
                    self.tt("dve", dv, s1, u, ALU.add, dbg=f"rope3 {h}.{tt}")
                else:
                    rr = V(xch, (0, 128), (1024, 1536))
                    self.tt("dve", rr, s1, u, ALU.add, dbg=f"rope3 {h}.{tt}")
                    aa = V(xch, (0, 128), (1536, 2048))
                    self.ts("dve", aa, rr, 0.0, None, ALU.min, dbg=f"elu a{h}.{tt}")
                    ee = V(xch, (0, 128), (0, 512))
                    self.act(ee, aa, AF.Exp, dbg=f"elu e{h}.{tt}")
                    bb = V(xch, (0, 128), (512, 1024))
                    self.ts("dve", bb, rr, 0.0, None, ALU.max, dbg=f"elu b{h}.{tt}")
                    self.tt("dve", dv, ee, bb, ALU.add, dbg=f"elu f{h}.{tt}")

    def attn1_projections(self):
        b, t, P, d = self.b, self.t, self.p, self.d
        V = Builder.v
        xln, ara, arc = t["ARBb"], t["ARAb"], t["ARCb"]
        b.barrier()
        b.dma("sp", V(arc, (0, 128), (12288, 13312)), V(d["c_ropecos"]), dbg="ropec1")
        b.dma("sp", V(arc, (0, 128), (14336, 15360)), V(d["c_ropesin"]), dbg="ropes1")
        vbufs = [t["SW0"], t["SW1"], t["SV0"], t["SV1"]]
        for qt in range(4):
            b.dma("pool", V(vbufs[qt], None, (0, 2048)),
                  V(d["l1_wvT"], (0, 128), (qt * 2048, (qt + 1) * 2048)), dbg=f"wv1{qt}")
        for tc in range(8):
            pb = P["PB4"] if tc % 2 == 0 else P["PB5"]
            for kc in range(DC):
                sw = vbufs[kc // 4]
                self.mm(V(pb, (0, 128), (0, 512)),
                        V(xln, (0, 128), (kc * 1024 + tc * 128, kc * 1024 + tc * 128 + 128)),
                        V(sw, (0, 128), ((kc % 4) * 512, (kc % 4) * 512 + 512)),
                        kc == 0, kc == DC - 1, dbg=f"v1 mm{tc}.{kc}")
            self.act(V(arc, (0, 128), (4096 + tc * 512, 4096 + (tc + 1) * 512)),
                     V(pb, (0, 128), (0, 512)), AF.Identity, dbg=f"v1 ev{tc}")
        self.proj_rope_heads(H, "l1_wqT", ara, 0, elu=True)
        self.proj_rope_heads(GKV, "l1_wkT", arc, 0, elu=True)
        self.k_to_tm()

    def attn1_scan(self, with_outputs):
        b, t, P = self.b, self.t, self.p
        V = Builder.v
        ara, arc = t["ARAb"], t["ARCb"]
        ard, ardb = t["ARD"], t["ARDb"]
        SB = 1032
        ZB = 1544
        obank = [P["PB2"], P["PB5"], P["PB6"], P["PB0"]]
        for c in range(NCH1):
            if with_outputs:
                for g in range(GKV):
                    self.act(V(ardb, (0, 128), (SB + g * 128, SB + (g + 1) * 128)),
                             V(ard, (0, 128), (g * 128, (g + 1) * 128)), AF.Identity, dbg=f"sbf{c}.{g}")
                    self.copy("dve", V(ardb, (0, 128), (ZB + g, ZB + g + 1)),
                              V(ard, (0, 128), (512 + g, 513 + g)), dbg=f"zbf{c}.{g}")
                for h in range(H):
                    g = h // 4
                    pb_sc = P["PB1"] if h % 2 == 0 else P["PB4"]
                    self.mm(V(pb_sc, (0, 128), (0, 128)),
                            V(arc, (0, 128), (g * 1024 + c * 128, g * 1024 + (c + 1) * 128)),
                            V(ara, (0, 128), (h * 1024 + c * 128, h * 1024 + (c + 1) * 128)),
                            True, True, dbg=f"s1c{c}.{h}")
                    self.tt("dve", V(ardb, (0, 128), (6144 + h * 128, 6144 + (h + 1) * 128)),
                            V(pb_sc, (0, 128), (0, 128)), V(t["trilT"]), ALU.mult, dbg=f"s1m{c}.{h}")
                for g in range(GKV):
                    ob = obank[g]
                    for j in range(4):
                        h = g * 4 + j
                        qsl = V(ara, (0, 128), (h * 1024 + c * 128, h * 1024 + (c + 1) * 128))
                        smv = V(ardb, (0, 128), (6144 + h * 128, 6144 + (h + 1) * 128))
                        self.mm(V(ob, (0, 128), (j * 128, (j + 1) * 128)),
                                V(arc, (0, 128), (4096 + c * 512 + g * 128, 4096 + c * 512 + (g + 1) * 128)),
                                smv, True, False, dbg=f"o1i{c}.{h}")
                        self.mm(V(ob, (0, 128), (j * 128, (j + 1) * 128)),
                                V(ardb, (0, 128), (SB + g * 128, SB + (g + 1) * 128)), qsl,
                                False, True, dbg=f"o1x{c}.{h}")
                        self.mm(V(P["PB3"], (0, 1), (j * 128, (j + 1) * 128)),
                                V(t["ones_col"], (0, 128), (0, 1)), smv, True, False, dbg=f"dn1{c}.{h}")
                        self.mm(V(P["PB3"], (0, 1), (j * 128, (j + 1) * 128)),
                                V(ardb, (0, 128), (ZB + g, ZB + g + 1)), qsl, False, True, dbg=f"dn2{c}.{h}")
                    dn = V(t["GFM"], (0, 1), (0, 512))
                    self.ts("dve", dn, V(P["PB3"], (0, 1), (0, 512)), ATT_EPS, None, ALU.add, dbg=f"dne{c}.{g}")
                    rc = V(t["GFM"], (0, 1), (512, 1024))
                    self.recip(rc, dn, dbg=f"dnr{c}.{g}")
                    rcb = V(t["CFM"], (0, 1), (0, 512))
                    self.copy("dve", rcb, rc, dbg=f"dnb{c}.{g}")
                    pbc = P["PB1"] if g % 2 == 0 else P["PB4"]
                    self.mm(V(pbc, (0, 128), (0, 512)), V(t["ones_row"], (0, 1), (0, 128)), rcb,
                            True, True, dbg=f"dnc{c}.{g}")
                    rbs = 2048 + (g % 2) * 1024
                    self.act(V(ardb, (0, 128), (rbs, rbs + 512)), V(pbc, (0, 128), (0, 512)),
                             AF.Identity, dbg=f"dnv{c}.{g}")
                    for j in range(4):
                        h = g * 4 + j
                        self.tt("dve", V(ara, (0, 128), (16384 + h * 1024 + c * 128, 16384 + h * 1024 + (c + 1) * 128)),
                                V(ob, (0, 128), (j * 128, (j + 1) * 128)),
                                V(ardb, (0, 128), (rbs + j * 128, rbs + (j + 1) * 128)), ALU.mult, dbg=f"o1f{c}.{h}")
            for g in range(GKV):
                ob = obank[g]
                self.mm(V(ob, (0, 128), (0, 128)),
                        V(arc, (0, 128), (8192 + c * 512 + g * 128, 8192 + c * 512 + (g + 1) * 128)),
                        V(arc, (0, 128), (4096 + c * 512 + g * 128, 4096 + c * 512 + (g + 1) * 128)),
                        True, True, dbg=f"d1{c}.{g}")
            for g in range(GKV):
                ob = obank[g]
                sfv = V(ard, (0, 128), (g * 128, (g + 1) * 128))
                self.tt("dve", sfv, sfv, V(ob, (0, 128), (0, 128)), ALU.add, dbg=f"su{c}.{g}")
                zt = V(t["GTM"], (0, 128), (g, g + 1))
                self.red(zt, V(arc, (0, 128), (g * 1024 + c * 128, g * 1024 + (c + 1) * 128)),
                         ALU.add, dbg=f"zr{c}.{g}")
                zf = V(ard, (0, 128), (512 + g, 513 + g))
                self.tt("dve", zf, zf, zt, ALU.add, dbg=f"zu{c}.{g}")

    def stage_attn1_part1(self):
        b, t = self.b, self.t
        V = Builder.v
        self.attn1_projections()
        ardv = V(t["ARD"], (0, 128), (0, 516))
        ap = Builder.ap(ardv)
        b.op("dve", lambda e, a=ap: e.memset(a, 0.0), writes=[ardv], dbg="sz0")
        self.attn1_scan(with_outputs=False)
        b.barrier()
        b.dma("sp", V(self.d["D1o"], (0, 128), (0, 516)), V(t["ARD"], (0, 128), (0, 516)), dbg="D1 out")
        b.dma("sp", V(self.d["q1o"]), V(t["ARAb"], (0, 128), (0, 16384)), dbg="q1 out")
        b.dma("sp", V(self.d["k1o"]), V(t["ARCb"], (0, 128), (0, 4096)), dbg="k1 out")
        b.dma("sp", V(self.d["v1o"]), V(t["ARCb"], (0, 128), (4096, 8192)), dbg="v1 out")

    def stage_attn1_part2(self):
        b, t, P, d = self.b, self.t, self.p, self.d
        V = Builder.v
        b.barrier()
        b.dma("sp", V(t["ARAb"], (0, 128), (0, 16384)), V(d["q1i"]), dbg="q1 in")
        b.dma("sp", V(t["ARCb"], (0, 128), (0, 4096)), V(d["k1i"]), dbg="k1 in")
        b.dma("sp", V(t["ARCb"], (0, 128), (4096, 8192)), V(d["v1i"]), dbg="v1 in")
        self.sbuf("coefL1", [128, 8], F32)
        ci = self.dram_in("c_coefL1", [128, 8], F32)
        b.dma("sp", V(t["coefL1"]), V(ci), dbg="coefL1")
        xch = t["XCH"]
        for j in range(NCORES):
            xv = V(xch, (0, 128), (0, 516))
            b.dma("sp", xv, V(d["D1all"], (j * 128, (j + 1) * 128), (0, 516)), dbg=f"D1 {j}")
            cc = V(t["coefL1"], (0, 128), (j, j + 1))
            full = V(xch, (0, 128), (0, 516))
            tgt = V(t["ARD"], (0, 128), (0, 516))
            if j == 0:
                self.ts("dve", tgt, full, cc, None, ALU.mult, dbg=f"sz seed{j}")
            else:
                self.stt("dve", tgt, full, cc, tgt, ALU.mult, ALU.add, dbg=f"sz seed{j}")
        self.k_to_tm()
        self.attn1_scan(with_outputs=True)
        b.barrier()
        for dc in range(DC):
            sw = t["SW0"] if dc % 2 == 0 else t["SW1"]
            b.dma("pool", V(sw), V(d["l1_woT"], (dc * 128, (dc + 1) * 128), (0, 2048)), dbg=f"wo1{dc}")
            xslot = (dc % 2) * 1024
            xv = V(t["XCH"], (0, 128), (xslot, xslot + 1024))
            b.dma("sp", xv, V(d["xh"], (dc * 128, (dc + 1) * 128), (0, TCORE)), dbg=f"a1res ld{dc}")
            for tt in range(NTT):
                po = P["PB3"] if tt == 0 else P["PB4"]
                pov = V(po, (0, 128), (0, 512))
                for kc in range(DC):
                    self.mm(pov, V(sw, (0, 128), (kc * 128, (kc + 1) * 128)),
                            V(t["ARAb"], (0, 128), (16384 + kc * 1024 + tt * 512, 16384 + kc * 1024 + tt * 512 + 512)),
                            kc == 0, kc == DC - 1, dbg=f"wo1 mm{dc}.{tt}.{kc}")
                xvt = V(t["XCH"], (0, 128), (xslot + tt * 512, xslot + tt * 512 + 512))
                self.tt("dve", xvt, xvt, pov, ALU.add, dbg=f"a1res {dc}.{tt}")
            b.dma("sp", V(d["xh"], (dc * 128, (dc + 1) * 128), (0, TCORE)), xv, dbg=f"a1res st{dc}")

    def finish(self):
        self.b.barrier()
        self.b.emit_all(self.es)
        self.es.close()
        return self.nc



def build_launch1(stages=("l0ffn",)):
    mk = MK(1)
    mk.dram_in("xT", [D, TCORE], F32)
    mk.dram_int("xh", [D, TCORE], F32)
    mk.dram_in("c_biasfm", [128, 184], F32)
    mk.dram_out("xo", [D, TCORE], F32)
    need_attn = "attn0p1" in stages
    if "l0ffn" in stages:
        mk.dram_in("l0_w1blk", [FC * 128, 2560], F32)
        mk.dram_in("l0_a1", [2 * 128, 2048], F32)
        mk.dram_in("l0_w2blk", [DC * 128, 8192], F32)
        mk.dram_in("l0_b2q", [2 * 128, 2048], F32)
        mk.dram_in("l0_wr", [128, 128], F32)
        mk.dram_in("l0_b2row", [1, 2048], F32)
        mk.dram_in("l0_br8", [8, 1], F32)
    if need_attn:
        mk.dram_in("l0_wqT", [16 * 128, 2048], F32)
        mk.dram_in("l0_wkT", [4 * 128, 2048], F32)
        mk.dram_in("l0_wvT", [128, 16 * 512], F32)
        mk.dram_out("D0o", [128, H * 128], F32)
        mk.dram_out("q0o", [128, 16384], BF16)
        mk.dram_out("k0o", [128, 4096], BF16)
        mk.dram_out("v0o", [128, 4096], BF16)
    mk.alloc_common()
    mk.load_small_tables(need_rope=need_attn, need_l0dec=need_attn)
    b = mk.b
    V = Builder.v
    b.dma("sp", V(mk.t["biasfm"]), V(mk.d["c_biasfm"]), dbg="ld biasfm")
    if "l0ffn" in stages:
        b.dma("sp", V(mk.t["br8"]), V(mk.d["l0_br8"]), dbg="ld br8")
    b.dma("sp", V(mk.d["xh"]), V(mk.d["xT"]), dbg="x->xh")
    if "l0ffn" in stages:
        mk.stage_ln()
        mk.stage_l0ffn()
    if need_attn:
        mk.stage_ln()
        mk.stage_attn0_part1()
    b.barrier()
    b.dma("sp", V(mk.d["xo"]), V(mk.d["xh"]), dbg="xh->xo")
    return mk.finish()


def make_biasfm(inp):
    bf = np.zeros((128, 184), np.float32)
    bf[:, 0:64] = _f32(inp["L0_ffn_b1"]).reshape(64, 128).T
    bup = _f32(inp["L1_ffn_bup"])
    bsw = _f32(inp["L1_ffn_bsw"])
    for n in range(NEX):
        bf[:, 64 + n * 8: 64 + (n + 1) * 8] = bup[n].reshape(8, 128).T
        bf[:, 112 + n * 8: 112 + (n + 1) * 8] = bsw[n].reshape(8, 128).T
    bf[:, 160:176] = _f32(inp["L1_ffn_bsu"]).reshape(16, 128).T
    bf[:, 176:184] = _f32(inp["L1_ffn_bsa"]).reshape(8, 128).T
    return bf


def base_const_inmap(core, need_rope=False, need_l0dec=False, need_l1mask=False):
    ht = host_tables()
    ct = core_tables(core)
    m = {
        "c_identb": np.asarray(ht["identb"]),
        "c_identf": np.asarray(ht["identf"]),
        "c_ones_col": np.asarray(ht["ones_col_bf"]),
        "c_ones_row": np.ascontiguousarray(np.ones((1, 512), ml_dtypes.bfloat16)),
        "c_onehotE": np.asarray(ht["onehotE"]),
        "c_onehotQ": np.asarray(ht["onehotQ"]),
    }
    if need_rope:
        m["c_ropecos"] = ct["ropecos"].astype(ml_dtypes.bfloat16)
        m["c_ropesin"] = ct["ropesin"].astype(ml_dtypes.bfloat16)
        m["c_Prot"] = ht_prot()
    if need_l0dec:
        m["c_diagT"] = np.asarray(ht["diagT"])
        m["c_qdec"] = np.asarray(ht["qdec"])
        m["c_kdeccol"] = np.asarray(ht["kdeccol"])
    if need_l1mask:
        m["c_trilT"] = np.asarray(ht["trilT"])
    return m


def ht_prot():
    p = np.zeros((128, 128), ml_dtypes.bfloat16)
    for pp in range(64):
        p[pp + 64, pp] = -1.0
        p[pp, pp + 64] = 1.0
    return p


def _decl_l0ffn(mk):
    mk.dram_in("l0_w1blk", [FC * 128, 2560], F32)
    mk.dram_in("l0_a1", [2 * 128, 2048], F32)
    mk.dram_in("l0_w2blk", [DC * 128, 8192], F32)
    mk.dram_in("l0_b2q", [2 * 128, 2048], F32)
    mk.dram_in("l0_wr", [128, 128], F32)
    mk.dram_in("l0_b2row", [1, 2048], F32)
    mk.dram_in("l0_br8", [8, 1], F32)


def _decl_l1ffn(mk):
    mk.dram_in("l1_wg", [2 * 128, 128], F32)
    mk.dram_in("l1_wup", [NEX * 8 * 128, 2048], F32)
    mk.dram_in("l1_wsw", [NEX * 8 * 128, 1024], F32)
    mk.dram_in("l1_wdn", [NEX * 16 * 128, 1024], F32)
    mk.dram_in("l1_wsu", [16 * 128, 2048], F32)
    mk.dram_in("l1_wsa", [8 * 128, 1024], F32)
    mk.dram_in("l1_wsd", [16 * 128, 2048], F32)
    mk.dram_in("l1_bdnD", [NEX * 8, 2048], F32)
    mk.dram_in("l1_bsdrow", [1, 2048], F32)
    mk.dram_in("l1_bg8", [8, 2], F32)


def build_launch2(stages=("attn0p2", "l0ffn", "l1ffn", "attn1p1")):
    mk = MK(2)
    mk.dram_in("xT", [D, TCORE], F32)
    mk.dram_int("xh", [D, TCORE], F32)
    mk.dram_in("c_biasfm", [128, 184], F32)
    mk.dram_out("xo", [D, TCORE], F32)
    if "attn0p2" in stages:
        mk.dram_in("q0i", [128, 16384], BF16)
        mk.dram_in("k0i", [128, 4096], BF16)
        mk.dram_in("v0i", [128, 4096], BF16)
        mk.dram_in("D0all", [NCORES * 128, 2048], F32)
        mk.dram_in("l0_woT", [16 * 128, 2048], F32)
    if "l0ffn" in stages:
        _decl_l0ffn(mk)
    if "l1ffn" in stages:
        _decl_l1ffn(mk)
    if "attn1p1" in stages:
        mk.dram_in("l1_wqT", [16 * 128, 2048], F32)
        mk.dram_in("l1_wkT", [4 * 128, 2048], F32)
        mk.dram_in("l1_wvT", [128, 16 * 512], F32)
        mk.dram_out("D1o", [128, 516], F32)
        mk.dram_out("q1o", [128, 16384], BF16)
        mk.dram_out("k1o", [128, 4096], BF16)
        mk.dram_out("v1o", [128, 4096], BF16)
    mk.alloc_common()
    mk.load_small_tables(need_rope="attn1p1" in stages, need_l0dec="attn0p2" in stages)
    b = mk.b
    V = Builder.v
    b.dma("sp", V(mk.t["biasfm"]), V(mk.d["c_biasfm"]), dbg="ld biasfm")
    if "l0ffn" in stages:
        b.dma("sp", V(mk.t["br8"]), V(mk.d["l0_br8"]), dbg="ld br8")
    if "l1ffn" in stages:
        b.dma("sp", V(mk.t["bg8"]), V(mk.d["l1_bg8"]), dbg="ld bg8")
    b.dma("sp", V(mk.d["xh"]), V(mk.d["xT"]), dbg="x->xh")
    if "attn0p2" in stages:
        mk.stage_attn0_part2()
    if "l0ffn" in stages:
        mk.stage_ln()
        mk.stage_l0ffn()
    if "l1ffn" in stages:
        mk.stage_ln()
        mk.stage_l1ffn()
    if "attn1p1" in stages:
        mk.stage_ln()
        mk.stage_attn1_part1()
    b.barrier()
    b.dma("sp", V(mk.d["xo"]), V(mk.d["xh"]), dbg="xh->xo")
    return mk.finish()


def build_launch3(stages=("attn1p2", "l1ffn")):
    mk = MK(3)
    mk.dram_in("xT", [D, TCORE], F32)
    mk.dram_int("xh", [D, TCORE], F32)
    mk.dram_in("c_biasfm", [128, 184], F32)
    mk.dram_out("xo", [D, TCORE], F32)
    if "attn1p2" in stages:
        mk.dram_in("q1i", [128, 16384], BF16)
        mk.dram_in("k1i", [128, 4096], BF16)
        mk.dram_in("v1i", [128, 4096], BF16)
        mk.dram_in("D1all", [NCORES * 128, 516], F32)
        mk.dram_in("l1_woT", [16 * 128, 2048], F32)
    if "l1ffn" in stages:
        _decl_l1ffn(mk)
    mk.alloc_common()
    mk.load_small_tables(need_l1mask="attn1p2" in stages)
    b = mk.b
    V = Builder.v
    b.dma("sp", V(mk.t["biasfm"]), V(mk.d["c_biasfm"]), dbg="ld biasfm")
    if "l1ffn" in stages:
        b.dma("sp", V(mk.t["bg8"]), V(mk.d["l1_bg8"]), dbg="ld bg8")
    b.dma("sp", V(mk.d["xh"]), V(mk.d["xT"]), dbg="x->xh")
    if "attn1p2" in stages:
        mk.stage_attn1_part2()
    if "l1ffn" in stages:
        mk.stage_ln()
        mk.stage_l1ffn()
    b.barrier()
    b.dma("sp", V(mk.d["xo"]), V(mk.d["xh"]), dbg="xh->xo")
    return mk.finish()



def kernel(**inp):
    w = pack_weights(inp)
    x = _f32(inp["L0_attn_x"])[0]
    xT = np.ascontiguousarray(x.T)
    biasfm = make_biasfm(inp)

    def shard(c):
        return np.ascontiguousarray(xT[:, c * TCORE:(c + 1) * TCORE])

    l0f = {k: w[k] for k in ("l0_w1blk", "l0_a1", "l0_w2blk", "l0_b2q", "l0_wr", "l0_b2row", "l0_br8")}
    l1f = {k: w[k] for k in ("l1_wg", "l1_wup", "l1_wsw", "l1_wdn", "l1_wsu", "l1_wsa", "l1_wsd",
                             "l1_bdnD", "l1_bsdrow", "l1_bg8")}

    nc1 = build_launch1(stages=("l0ffn", "attn0p1"))
    maps = []
    for c in range(NCORES):
        m = base_const_inmap(c, need_rope=True, need_l0dec=True)
        m["xT"] = shard(c)
        m["c_biasfm"] = biasfm
        m.update(l0f)
        m["l0_wqT"] = w["l0_wqT"]
        m["l0_wkT"] = w["l0_wkT"]
        m["l0_wvT"] = w["l0_wvT"]
        maps.append(m)
    r1 = run_bass_kernel_spmd(nc1, maps, core_ids=list(range(NCORES)))
    D0all = np.concatenate([r1.results[c]["D0o"] for c in range(NCORES)], axis=0)

    nc2 = build_launch2()
    maps = []
    for c in range(NCORES):
        m = base_const_inmap(c, need_rope=True, need_l0dec=True)
        m["xT"] = r1.results[c]["xo"]
        m["c_biasfm"] = biasfm
        m["q0i"] = r1.results[c]["q0o"]
        m["k0i"] = r1.results[c]["k0o"]
        m["v0i"] = r1.results[c]["v0o"]
        m["D0all"] = D0all
        m["c_coefL0"] = core_tables(c)["coefL0"]
        m["l0_woT"] = w["l0_woT"]
        m.update(l0f)
        m.update(l1f)
        m["l1_wqT"] = w["l1_wqT"]
        m["l1_wkT"] = w["l1_wkT"]
        m["l1_wvT"] = w["l1_wvT"]
        maps.append(m)
    r2 = run_bass_kernel_spmd(nc2, maps, core_ids=list(range(NCORES)))
    D1all = np.concatenate([r2.results[c]["D1o"] for c in range(NCORES)], axis=0)

    nc3 = build_launch3()
    maps = []
    for c in range(NCORES):
        m = base_const_inmap(c)
        m["c_trilT"] = np.asarray(host_tables()["trilT"])
        m["xT"] = r2.results[c]["xo"]
        m["c_biasfm"] = biasfm
        m["q1i"] = r2.results[c]["q1o"]
        m["k1i"] = r2.results[c]["k1o"]
        m["v1i"] = r2.results[c]["v1o"]
        m["D1all"] = D1all
        m["c_coefL1"] = core_tables(c)["coefL1"]
        m["l1_woT"] = w["l1_woT"]
        m.update(l1f)
        maps.append(m)
    r3 = run_bass_kernel_spmd(nc3, maps, core_ids=list(range(NCORES)))

    out = np.concatenate([r3.results[c]["xo"] for c in range(NCORES)], axis=1)
    return np.ascontiguousarray(out.T)[None].astype(np.float32)
